# version 01 (final)
# baseline (speedup 1.0000x reference)
import numpy as np
import ml_dtypes
from contextlib import ExitStack

import concourse.bass as bass
import concourse.mybir as mybir
from concourse.bass_utils import run_bass_kernel_spmd

F32 = mybir.dt.float32
BF16 = mybir.dt.bfloat16
AF = mybir.ActivationFunctionType
ALU = mybir.AluOpType
AX = mybir.AxisListType
BF = ml_dtypes.bfloat16

NC = 8
D = 2048
TOK = 8192
TPC = 1024
S = 2048
EPS = 1e-5


class Prog:
    ENGS = ("sync", "tensor", "vector", "scalar", "gpsimd")

    def __init__(self):
        self.nc = bass.Bass("TRN2", target_bir_lowering=False)
        self.items = []
        self.v = 0

    def _add(self, eng, fn, inc):
        self.items.append((eng, self.v, fn, inc))
        self.v += inc

    def dma(self, pairs):
        pairs = list(pairs)

        def fn(e, pairs=pairs):
            return [e.dma_start(out=o, in_=i) for (o, i) in pairs]

        self._add("sync", fn, 16 * len(pairs))

    def mm(self, lst):
        lst = list(lst)

        def fn(e, lst=lst):
            r = None
            for (o, l, rr, st, sp) in lst:
                r = e.matmul(o, l, rr, start=st, stop=sp)
            return r

        self._add("tensor", fn, 1)

    def tr(self, lst):
        lst = list(lst)

        def fn(e, lst=lst):
            r = None
            for (o, i, idn) in lst:
                r = e.transpose(o, i, idn)
            return r

        self._add("tensor", fn, 1)

    def op(self, eng, meth, *a, **k):
        def fn(e, m=meth, a=a, k=k):
            return getattr(e, m)(*a, **k)

        self._add(eng, fn, 1)

    def finalize(self, st):
        nc = self.nc
        sem = st.enter_context(nc.semaphore("S"))
        blk = st.enter_context(nc.Block())
        used = {e for (e, _, _, _) in self.items} | {"sync"}
        for name in self.ENGS:
            if name not in used:
                continue

            def runner(eng, name=name, sem=sem):
                for (en, wait_at, fn, inc) in self.items:
                    if en != name:
                        continue
                    eng.wait_ge(sem, wait_at)
                    r = fn(eng)
                    if isinstance(r, list):
                        for inst in r:
                            inst.then_inc(sem, 16)
                    else:
                        r.then_inc(sem, inc)
                if name == "sync":
                    eng.wait_ge(sem, self.v)

            getattr(blk, name)(runner)
        st.close()
        return nc


def layernorm(p, out, xin, sq, msum, m, vs, sd, rstd, dim, eps_ap):
    p.op("vector", "reduce_sum", msum, xin, AX.X)
    p.op("vector", "tensor_scalar", m, msum, 1.0 / dim, None, ALU.mult)
    p.op("vector", "tensor_scalar", out, xin, m, None, ALU.subtract)
    p.op("scalar", "activation", sq, out, AF.Square)
    p.op("vector", "reduce_sum", vs, sq, AX.X)
    p.op("vector", "tensor_scalar", vs, vs, 1.0 / dim, None, ALU.mult)
    p.op("vector", "tensor_tensor", vs, vs, eps_ap, ALU.add)
    p.op("scalar", "activation", sd, vs, AF.Sqrt)
    p.op("vector", "reciprocal", rstd, sd)
    p.op("vector", "tensor_scalar", out, out, rstd, None, ALU.mult)


def elup1(p, out, xin, scratch):
    p.op("vector", "tensor_scalar", out, xin, 0.0, None, ALU.min)
    p.op("scalar", "activation", out, out, AF.Exp)
    p.op("vector", "tensor_scalar", scratch, xin, 0.0, None, ALU.max)
    p.op("vector", "tensor_tensor", out, out, scratch, ALU.add)


def build_A():
    p = Prog()
    nc = p.nc
    st = ExitStack()
    sb = lambda n, s, d: st.enter_context(nc.sbuf_tensor(n, s, d))
    ps = lambda n, s, d: st.enter_context(nc.psum_tensor(n, s, d))

    x = nc.dram_tensor("x", [TPC, D], F32, kind="ExternalInput").ap()
    cos = nc.dram_tensor("cos", [TPC, D], F32, kind="ExternalInput").ap()
    sin = nc.dram_tensor("sin", [TPC, D], F32, kind="ExternalInput").ap()
    wq = nc.dram_tensor("wqt", [D, D], F32, kind="ExternalInput").ap()
    wk = nc.dram_tensor("wkt", [D, D], F32, kind="ExternalInput").ap()
    wv = nc.dram_tensor("wvt", [D, D], F32, kind="ExternalInput").ap()
    idb = nc.dram_tensor("identb", [128, 128], BF16, kind="ExternalInput").ap()
    qo = nc.dram_tensor("qo", [TPC, D], F32, kind="ExternalOutput").ap()
    ko = nc.dram_tensor("ko", [TPC, D], F32, kind="ExternalOutput").ap()
    vo = nc.dram_tensor("vo", [TPC, D], F32, kind="ExternalOutput").ap()

    xt = sb("xt", [128, D], F32)
    ln = sb("ln", [128, D], F32)
    sq = sb("sq", [128, D], F32)
    msum = sb("msum", [128, 1], F32)
    m = sb("m", [128, 1], F32)
    vs = sb("vs", [128, 1], F32)
    sd = sb("sd", [128, 1], F32)
    rstd = sb("rstd", [128, 1], F32)
    eps_sb = sb("eps_sb", [128, 1], F32)
    ln_bf = sb("ln_bf", [128, D], BF16)
    lnT = sb("lnT", [128, 16, TPC], BF16)
    idb_sb = sb("idb_sb", [128, 128], BF16)
    wtmp = sb("wtmp", [128, D], F32)
    wbf = sb("wbf", [128, 16, D], BF16)
    q_sb = sb("q_sb", [128, D], F32)
    qr = sb("qr", [128, D], F32)
    tmp = sb("tmp", [128, D], F32)
    cos_sb = sb("cos_sb", [128, D], F32)
    sin_sb = sb("sin_sb", [128, D], F32)

    q_ps = ps("q_ps", [128, D], F32)
    tp_ps = ps("tp_ps", [128, 16, 128], BF16)

    p.dma([(idb_sb[:, :], idb)])
    p.op("vector", "memset", eps_sb[:, :], EPS)
    for tt in range(8):
        p.dma([(xt[:, :], x[tt * 128:(tt + 1) * 128, :])])
        layernorm(p, ln[:, :], xt[:, :], sq[:, :], msum[:, :], m[:, :],
                  vs[:, :], sd[:, :], rstd[:, :], D, eps_sb[:, :])
        p.op("vector", "tensor_copy", ln_bf[:, :], ln[:, :])
        p.tr([(tp_ps[:, j, :], ln_bf[:, j * 128:(j + 1) * 128], idb_sb[:, :])
              for j in range(16)])
        p.op("vector", "tensor_copy", lnT[:, :, tt * 128:(tt + 1) * 128], tp_ps[:, :, :])

    for (wdram, outdram, do_rope) in ((wq, qo, True), (wk, ko, True), (wv, vo, False)):
        wr = wdram.rearrange("(a p) d -> p a d", p=128)
        for i in range(16):
            p.dma([(wtmp[:, :], wr[:, i, :])])
            p.op("vector", "tensor_copy", wbf[:, i, :], wtmp[:, :])
        for tt in range(8):
            for ch in range(4):
                p.mm([(q_ps[:, ch * 512:(ch + 1) * 512],
                       lnT[:, ht, tt * 128:(tt + 1) * 128],
                       wbf[:, ht, ch * 512:(ch + 1) * 512],
                       ht == 0, ht == 15) for ht in range(16)])
            p.op("vector", "tensor_copy", q_sb[:, :], q_ps[:, :])
            if do_rope:
                p.dma([(cos_sb[:, :], cos[tt * 128:(tt + 1) * 128, :]),
                       (sin_sb[:, :], sin[tt * 128:(tt + 1) * 128, :])])
                H2 = D // 2
                p.op("vector", "tensor_tensor", qr[:, 0:H2], q_sb[:, 0:H2], cos_sb[:, 0:H2], ALU.mult)
                p.op("vector", "tensor_tensor", tmp[:, 0:H2], q_sb[:, H2:D], sin_sb[:, 0:H2], ALU.mult)
                p.op("vector", "tensor_tensor", qr[:, 0:H2], qr[:, 0:H2], tmp[:, 0:H2], ALU.subtract)
                p.op("vector", "tensor_tensor", qr[:, H2:D], q_sb[:, H2:D], cos_sb[:, H2:D], ALU.mult)
                p.op("vector", "tensor_tensor", tmp[:, 0:H2], q_sb[:, 0:H2], sin_sb[:, H2:D], ALU.mult)
                p.op("vector", "tensor_tensor", qr[:, H2:D], qr[:, H2:D], tmp[:, 0:H2], ALU.add)
                p.dma([(outdram[tt * 128:(tt + 1) * 128, :], qr[:, :])])
            else:
                p.dma([(outdram[tt * 128:(tt + 1) * 128, :], q_sb[:, :])])
    return p.finalize(st)


def build_B():
    p = Prog()
    nc = p.nc
    st = ExitStack()
    sb = lambda n, s, d: st.enter_context(nc.sbuf_tensor(n, s, d))
    ps = lambda n, s, d: st.enter_context(nc.psum_tensor(n, s, d))
    SCALE = 1.0 / np.sqrt(128.0)

    qT = nc.dram_tensor("qT", [8 * 128, S], F32, kind="ExternalInput").ap()
    kT = nc.dram_tensor("kT", [8 * 128, S], F32, kind="ExternalInput").ap()
    kN = nc.dram_tensor("kN", [8 * S, 128], F32, kind="ExternalInput").ap()
    vN = nc.dram_tensor("vN", [8 * S, 128], F32, kind="ExternalInput").ap()
    gate = nc.dram_tensor("gate", [8 * 128, 1], F32, kind="ExternalInput").ap()
    idb = nc.dram_tensor("identb", [128, 128], BF16, kind="ExternalInput").ap()
    idf = nc.dram_tensor("identf", [128, 128], F32, kind="ExternalInput").ap()
    comb = nc.dram_tensor("comb", [8 * S, 128], F32, kind="ExternalOutput").ap()

    qT_f = sb("qT_f", [128, S], F32)
    kT_f = sb("kT_f", [128, S], F32)
    kn = sb("kn", [128, 16, 128], F32)
    vn = sb("vn", [128, 16, 128], F32)
    qT_bf = sb("qT_bf", [128, S], BF16)
    kT_bf = sb("kT_bf", [128, S], BF16)
    v_bf = sb("v_bf", [128, 16, 128], BF16)
    kf = sb("kf", [128, 16, 128], F32)
    qf = sb("qf", [128, S], F32)
    scr = sb("scr", [128, S], F32)
    scr3 = sb("scr3", [128, 16, 128], F32)
    scr2 = sb("scr2", [128, 128], F32)
    mem_rhs = sb("mem_rhs", [128, 129], F32)
    ones = sb("ones", [128, 1], F32)
    g_in = sb("g_in", [128, 1], F32)
    g_sb = sb("g_sb", [128, 1], F32)
    P_sb = sb("P_sb", [128, S], BF16)
    PT_sb = sb("PT_sb", [128, 16, 128], BF16)
    rm = sb("rm", [128, 1], F32)
    rmb = sb("rmb", [128, 1], F32)
    den = sb("den", [128, 1], F32)
    dn = sb("dn", [128, 1], F32)
    dr = sb("dr", [128, 1], F32)
    mo_sb = sb("mo_sb", [128, 128], F32)
    attn_sb = sb("attn_sb", [128, 128], F32)
    dsb = sb("dsb", [128, 128], F32)
    comb_sb = sb("comb_sb", [128, 16, 128], F32)
    idb_sb = sb("idb_sb", [128, 128], BF16)
    idf_sb = sb("idf_sb", [128, 128], F32)

    sc_ps = ps("sc_ps", [128, S], F32)
    tp_ps = ps("tp_ps", [128, 16, 128], BF16)
    misc = ps("misc", [128, 512], F32)

    p.dma([(idb_sb[:, :], idb), (idf_sb[:, :], idf)])
    p.op("vector", "memset", ones[:, :], 1.0)

    for pr in range(8):
        p.dma([
            (qT_f[:, :], qT[pr * 128:(pr + 1) * 128, :]),
            (kT_f[:, :], kT[pr * 128:(pr + 1) * 128, :]),
            (kn[:, :, :], kN[pr * S:(pr + 1) * S, :].rearrange("(t q) e -> q t e", q=128)),
            (vn[:, :, :], vN[pr * S:(pr + 1) * S, :].rearrange("(t q) e -> q t e", q=128)),
            (g_in[:, :], gate[pr * 128:(pr + 1) * 128, :]),
        ])
        p.op("vector", "tensor_copy", qT_bf[:, :], qT_f[:, :])
        p.op("vector", "tensor_copy", kT_bf[:, :], kT_f[:, :])
        p.op("vector", "tensor_copy", v_bf[:, :, :], vn[:, :, :])
        p.op("scalar", "activation", g_sb[:, :], g_in[:, :], AF.Sigmoid)
        elup1(p, kf[:, :, :], kn[:, :, :], scr3[:, :, :])
        elup1(p, qf[:, :], qT_f[:, :], scr[:, :])
        p.mm([(misc[:, 0:128], vn[:, i, :], kf[:, i, :], i == 0, i == 15) for i in range(16)])
        p.mm([(misc[:, 128:129], kf[:, i, :], ones[:, :], i == 0, i == 15) for i in range(16)])
        p.op("vector", "tensor_copy", scr2[:, :], misc[:, 0:128])
        p.tr([(misc[:, 256:384], scr2[:, :], idf_sb[:, :])])
        p.op("vector", "tensor_copy", mem_rhs[:, 0:128], misc[:, 256:384])
        p.op("vector", "tensor_copy", mem_rhs[:, 128:129], misc[:, 128:129])

        for c in range(16):
            cs = slice(c * 128, (c + 1) * 128)
            p.mm([(misc[:, 0:129], qf[:, cs], mem_rhs[:, 0:129], True, True)])
            p.op("vector", "reciprocal", dn[:, :], misc[:, 128:129])
            p.op("vector", "tensor_scalar", mo_sb[:, :], misc[:, 0:128], dn[:, :], None, ALU.mult)
            p.mm([(sc_ps[:, j * 512:(j + 1) * 512], qT_bf[:, cs],
                   kT_bf[:, j * 512:(j + 1) * 512], True, True) for j in range(4)])
            p.op("vector", "reduce_max", rm[:, :], sc_ps[:, :], AX.X, negate=True)
            p.op("vector", "tensor_scalar", rmb[:, :], rm[:, :], SCALE, None, ALU.mult)
            p.op("scalar", "activation", P_sb[:, :], sc_ps[:, :], AF.Exp,
                 bias=rmb[:, :], scale=SCALE)
            p.op("vector", "reduce_sum", den[:, :], P_sb[:, :], AX.X)
            p.tr([(tp_ps[:, j, :], P_sb[:, j * 128:(j + 1) * 128], idb_sb[:, :])
                  for j in range(16)])
            p.op("vector", "tensor_copy", PT_sb[:, :, :], tp_ps[:, :, :])
            p.mm([(misc[:, 256:384], PT_sb[:, j, :], v_bf[:, j, :], j == 0, j == 15)
                  for j in range(16)])
            p.op("vector", "reciprocal", dr[:, :], den[:, :])
            p.op("vector", "tensor_scalar", attn_sb[:, :], misc[:, 256:384], dr[:, :], None, ALU.mult)
            p.op("vector", "tensor_tensor", dsb[:, :], mo_sb[:, :], attn_sb[:, :], ALU.subtract)
            p.op("vector", "tensor_scalar", dsb[:, :], dsb[:, :], g_sb[:, :], None, ALU.mult)
            p.op("vector", "tensor_tensor", comb_sb[:, c, :], attn_sb[:, :], dsb[:, :], ALU.add)
        p.dma([(comb[pr * S:(pr + 1) * S, :].rearrange("(t q) e -> q t e", q=128),
                comb_sb[:, :, :])])
    return p.finalize(st)


def build_C():
    p = Prog()
    nc = p.nc
    st = ExitStack()
    sb = lambda n, s, d: st.enter_context(nc.sbuf_tensor(n, s, d))
    ps = lambda n, s, d: st.enter_context(nc.psum_tensor(n, s, d))

    cT = nc.dram_tensor("combT", [D, TPC], F32, kind="ExternalInput").ap()
    wot = nc.dram_tensor("wot", [D, D], F32, kind="ExternalInput").ap()
    x = nc.dram_tensor("x", [TPC, D], F32, kind="ExternalInput").ap()
    idb = nc.dram_tensor("identb", [128, 128], BF16, kind="ExternalInput").ap()
    x1 = nc.dram_tensor("x1", [TPC, D], F32, kind="ExternalOutput").ap()
    ln1t = nc.dram_tensor("ln1t", [D, TPC], BF16, kind="ExternalOutput").ap()

    ctmp = sb("ctmp", [128, TPC], F32)
    cbf = sb("cbf", [128, 16, TPC], BF16)
    wtmp = sb("wtmp", [128, D], F32)
    wbf = sb("wbf", [128, 16, D], BF16)
    xt = sb("xt", [128, D], F32)
    x1_sb = sb("x1_sb", [128, D], F32)
    ln = sb("ln", [128, D], F32)
    sq = sb("sq", [128, D], F32)
    msum = sb("msum", [128, 1], F32)
    m = sb("m", [128, 1], F32)
    vs = sb("vs", [128, 1], F32)
    sd = sb("sd", [128, 1], F32)
    rstd = sb("rstd", [128, 1], F32)
    eps_sb = sb("eps_sb", [128, 1], F32)
    ln_bf = sb("ln_bf", [128, D], BF16)
    lt_sb = sb("lt_sb", [128, 16, 128], BF16)
    idb_sb = sb("idb_sb", [128, 128], BF16)

    o_ps = ps("o_ps", [128, D], F32)
    tp_ps = ps("tp_ps", [128, 16, 128], BF16)

    p.dma([(idb_sb[:, :], idb)])
    p.op("vector", "memset", eps_sb[:, :], EPS)
    for i in range(16):
        p.dma([(ctmp[:, :], cT[i * 128:(i + 1) * 128, :])])
        p.op("vector", "tensor_copy", cbf[:, i, :], ctmp[:, :])
    wr = wot.rearrange("(a p) d -> p a d", p=128)
    for i in range(16):
        p.dma([(wtmp[:, :], wr[:, i, :])])
        p.op("vector", "tensor_copy", wbf[:, i, :], wtmp[:, :])

    ln1t_r = ln1t.rearrange("(a p) t -> p a t", p=128)
    for tt in range(8):
        for ch in range(4):
            p.mm([(o_ps[:, ch * 512:(ch + 1) * 512],
                   cbf[:, et, tt * 128:(tt + 1) * 128],
                   wbf[:, et, ch * 512:(ch + 1) * 512],
                   et == 0, et == 15) for et in range(16)])
        p.dma([(xt[:, :], x[tt * 128:(tt + 1) * 128, :])])
        p.op("vector", "tensor_tensor", x1_sb[:, :], xt[:, :], o_ps[:, :], ALU.add)
        p.dma([(x1[tt * 128:(tt + 1) * 128, :], x1_sb[:, :])])
        layernorm(p, ln[:, :], x1_sb[:, :], sq[:, :], msum[:, :], m[:, :],
                  vs[:, :], sd[:, :], rstd[:, :], D, eps_sb[:, :])
        p.op("vector", "tensor_copy", ln_bf[:, :], ln[:, :])
        p.tr([(tp_ps[:, j, :], ln_bf[:, j * 128:(j + 1) * 128], idb_sb[:, :])
              for j in range(16)])
        p.op("vector", "tensor_copy", lt_sb[:, :, :], tp_ps[:, :, :])
        p.dma([(ln1t_r[:, :, tt * 128:(tt + 1) * 128], lt_sb[:, :, :])])
    return p.finalize(st)


def build_D():
    p = Prog()
    nc = p.nc
    st = ExitStack()
    sb = lambda n, s, d: st.enter_context(nc.sbuf_tensor(n, s, d))
    ps = lambda n, s, d: st.enter_context(nc.psum_tensor(n, s, d))

    pg = nc.dram_tensor("pg", [4 * D, TPC], F32, kind="ExternalInput").ap()
    cg = nc.dram_tensor("cg", [4 * D, TPC], BF16, kind="ExternalInput").ap()
    pu = nc.dram_tensor("pu", [4 * D, TPC], F32, kind="ExternalInput").ap()
    pd = nc.dram_tensor("pd", [4 * TPC, D], F32, kind="ExternalInput").ap()
    cd = nc.dram_tensor("cd", [4 * TPC, D], BF16, kind="ExternalInput").ap()
    lnt = nc.dram_tensor("lnt", [D, TOK], BF16, kind="ExternalInput").ap()
    part = nc.dram_tensor("part", [TOK, D], F32, kind="ExternalOutput").ap()

    pbuf = [sb(f"pb{k}", [128, D], F32) for k in range(4)]
    cbuf = [sb(f"cb{k}", [128, D], BF16) for k in range(4)]
    cf = sb("cf", [128, D], F32)
    acc = sb("acc", [128, D], F32)
    tmp = sb("tmp", [128, D], F32)
    wg = sb("wg", [128, 16, TPC], BF16)
    wu = sb("wu", [128, 16, TPC], BF16)
    wd = sb("wd", [128, 8, D], BF16)
    lnb = sb("lnb", [128, 16, 512], BF16)
    sil = sb("sil", [128, 512], F32)
    inter = sb("inter", [128, 8, 512], BF16)
    outsb = sb("outsb", [128, D], F32)

    g_ps = ps("g_ps", [128, 512], F32)
    u_ps = ps("u_ps", [128, 512], F32)
    o_ps = ps("o_ps", [128, D], F32)

    def recon(pdram, cdram, wdst, ht, width):
        p.dma([(pbuf[k][:, 0:width], pdram[k * (pdram.shape[0] // 4) + ht * 128:
                                           k * (pdram.shape[0] // 4) + (ht + 1) * 128, :])
               for k in range(4)])
        p.dma([(cbuf[k][:, 0:width], cdram[k * (cdram.shape[0] // 4) + ht * 128:
                                           k * (cdram.shape[0] // 4) + (ht + 1) * 128, :])
               for k in range(4)])
        p.op("vector", "tensor_copy", cf[:, 0:width], cbuf[0][:, 0:width])
        p.op("vector", "tensor_tensor", acc[:, 0:width], pbuf[0][:, 0:width], cf[:, 0:width], ALU.mult)
        for k in range(1, 4):
            p.op("vector", "tensor_copy", cf[:, 0:width], cbuf[k][:, 0:width])
            p.op("vector", "tensor_tensor", tmp[:, 0:width], pbuf[k][:, 0:width], cf[:, 0:width], ALU.mult)
            p.op("vector", "tensor_tensor", acc[:, 0:width], acc[:, 0:width], tmp[:, 0:width], ALU.add)
        p.op("vector", "tensor_copy", wdst, acc[:, 0:width])

    for ht in range(16):
        recon(pg, cg, wg[:, ht, :], ht, TPC)
    for ht in range(16):
        recon(pu, cg, wu[:, ht, :], ht, TPC)
    for it in range(8):
        recon(pd, cd, wd[:, it, :], it, D)

    lnt_r = lnt.rearrange("(a p) t -> p a t", p=128)
    for ck in range(16):
        tsl = slice(ck * 512, (ck + 1) * 512)
        p.dma([(lnb[:, :, :], lnt_r[:, :, tsl])])
        for ic in range(8):
            ics = slice(ic * 128, (ic + 1) * 128)
            p.mm([(g_ps[:, :], wg[:, ht, ics], lnb[:, ht, :], ht == 0, ht == 15)
                  for ht in range(16)])
            p.mm([(u_ps[:, :], wu[:, ht, ics], lnb[:, ht, :], ht == 0, ht == 15)
                  for ht in range(16)])
            p.op("scalar", "activation", sil[:, :], g_ps[:, :], AF.Silu)
            p.op("vector", "tensor_tensor", inter[:, ic, :], sil[:, :], u_ps[:, :], ALU.mult)
        for tc in range(4):
            for hc in range(4):
                p.mm([(o_ps[:, hc * 512:(hc + 1) * 512],
                       inter[:, it, tc * 128:(tc + 1) * 128],
                       wd[:, it, hc * 512:(hc + 1) * 512],
                       it == 0, it == 7) for it in range(8)])
            p.op("vector", "tensor_copy", outsb[:, :], o_ps[:, :])
            r0 = ck * 512 + tc * 128
            p.dma([(part[r0:r0 + 128, :], outsb[:, :])])
    return p.finalize(st)


def build_EF():
    p = Prog()
    nc = p.nc
    st = ExitStack()
    sb = lambda n, s, d: st.enter_context(nc.sbuf_tensor(n, s, d))
    ps = lambda n, s, d: st.enter_context(nc.psum_tensor(n, s, d))
    QKV = 3072

    x1 = nc.dram_tensor("x1", [TPC, D], F32, kind="ExternalInput").ap()
    parts = nc.dram_tensor("parts", [8 * TPC, D], F32, kind="ExternalInput").ap()
    wqkv = nc.dram_tensor("wqkv", [D, QKV], F32, kind="ExternalInput").ap()
    bqkv = nc.dram_tensor("bqkv", [128, 24], F32, kind="ExternalInput").ap()
    idb = nc.dram_tensor("identb", [128, 128], BF16, kind="ExternalInput").ap()
    x2 = nc.dram_tensor("x2", [TPC, D], F32, kind="ExternalOutput").ap()
    qkvt = nc.dram_tensor("qkvt", [QKV, TPC], F32, kind="ExternalOutput").ap()

    acc = sb("acc", [128, D], F32)
    ptmp = sb("ptmp", [128, D], F32)
    ln = sb("ln", [128, D], F32)
    sq = sb("sq", [128, D], F32)
    msum = sb("msum", [128, 1], F32)
    m = sb("m", [128, 1], F32)
    vs = sb("vs", [128, 1], F32)
    sd = sb("sd", [128, 1], F32)
    rstd = sb("rstd", [128, 1], F32)
    eps_sb = sb("eps_sb", [128, 1], F32)
    ln_bf = sb("ln_bf", [128, D], BF16)
    lnT = sb("lnT", [128, 16, TPC], BF16)
    idb_sb = sb("idb_sb", [128, 128], BF16)
    wtmp = sb("wtmp", [128, QKV], F32)
    wbf = sb("wbf", [128, 16, QKV], BF16)
    b_sb = sb("b_sb", [128, 24], F32)
    q_sb = sb("q_sb", [128, TPC], F32)

    qk_ps = ps("qk_ps", [128, TPC], F32)
    tp_ps = ps("tp_ps", [128, 16, 128], BF16)

    p.dma([(idb_sb[:, :], idb), (b_sb[:, :], bqkv)])
    p.op("vector", "memset", eps_sb[:, :], EPS)
    for tt in range(8):
        p.dma([(acc[:, :], x1[tt * 128:(tt + 1) * 128, :])])
        for d_ in range(8):
            p.dma([(ptmp[:, :], parts[d_ * TPC + tt * 128:d_ * TPC + (tt + 1) * 128, :])])
            p.op("vector", "tensor_tensor", acc[:, :], acc[:, :], ptmp[:, :], ALU.add)
        p.dma([(x2[tt * 128:(tt + 1) * 128, :], acc[:, :])])
        layernorm(p, ln[:, :], acc[:, :], sq[:, :], msum[:, :], m[:, :],
                  vs[:, :], sd[:, :], rstd[:, :], D, eps_sb[:, :])
        p.op("vector", "tensor_copy", ln_bf[:, :], ln[:, :])
        p.tr([(tp_ps[:, j, :], ln_bf[:, j * 128:(j + 1) * 128], idb_sb[:, :])
              for j in range(16)])
        p.op("vector", "tensor_copy", lnT[:, :, tt * 128:(tt + 1) * 128], tp_ps[:, :, :])

    wr = wqkv.rearrange("(a p) f -> p a f", p=128)
    for i in range(16):
        p.dma([(wtmp[:, :], wr[:, i, :])])
        p.op("vector", "tensor_copy", wbf[:, i, :], wtmp[:, :])
    for fc in range(24):
        fs = slice(fc * 128, (fc + 1) * 128)
        for half in range(2):
            hs = slice(half * 512, (half + 1) * 512)
            p.mm([(qk_ps[:, hs], wbf[:, dt, fs], lnT[:, dt, hs], dt == 0, dt == 15)
                  for dt in range(16)])
        p.op("vector", "tensor_scalar", q_sb[:, :], qk_ps[:, :], b_sb[:, fc:fc + 1], None, ALU.add)
        p.dma([(qkvt[fc * 128:(fc + 1) * 128, :], q_sb[:, :])])
    return p.finalize(st)


def build_G():
    p = Prog()
    nc = p.nc
    st = ExitStack()
    sb = lambda n, s, d: st.enter_context(nc.sbuf_tensor(n, s, d))
    ps = lambda n, s, d: st.enter_context(nc.psum_tensor(n, s, d))
    HD = 64
    SCALE = 0.125

    qT = nc.dram_tensor("qT", [16 * HD, S], F32, kind="ExternalInput").ap()
    kT = nc.dram_tensor("kT", [16 * HD, S], F32, kind="ExternalInput").ap()
    vN = nc.dram_tensor("vN", [16 * S, HD], F32, kind="ExternalInput").ap()
    idb = nc.dram_tensor("identb", [128, 128], BF16, kind="ExternalInput").ap()
    comb = nc.dram_tensor("comb", [16 * S, HD], F32, kind="ExternalOutput").ap()

    qT_f = sb("qT_f", [64, S], F32)
    kT_f = sb("kT_f", [64, S], F32)
    vn = sb("vn", [128, 16, HD], F32)
    qT_bf = sb("qT_bf", [64, S], BF16)
    kT_bf = sb("kT_bf", [64, S], BF16)
    v_bf = sb("v_bf", [128, 16, HD], BF16)
    P_sb = sb("P_sb", [128, S], BF16)
    PT_sb = sb("PT_sb", [128, 16, 128], BF16)
    rm = sb("rm", [128, 1], F32)
    rmb = sb("rmb", [128, 1], F32)
    den = sb("den", [128, 1], F32)
    dr = sb("dr", [128, 1], F32)
    comb_sb = sb("comb_sb", [128, 16, HD], F32)
    idb_sb = sb("idb_sb", [128, 128], BF16)

    sc_ps = ps("sc_ps", [128, S], F32)
    tp_ps = ps("tp_ps", [128, 16, 128], BF16)
    av_ps = ps("av_ps", [128, HD], F32)

    p.dma([(idb_sb[:, :], idb)])
    for pr in range(16):
        p.dma([
            (qT_f[:, :], qT[pr * HD:(pr + 1) * HD, :]),
            (kT_f[:, :], kT[pr * HD:(pr + 1) * HD, :]),
            (vn[:, :, :], vN[pr * S:(pr + 1) * S, :].rearrange("(t q) e -> q t e", q=128)),
        ])
        p.op("vector", "tensor_copy", qT_bf[:, :], qT_f[:, :])
        p.op("vector", "tensor_copy", kT_bf[:, :], kT_f[:, :])
        p.op("vector", "tensor_copy", v_bf[:, :, :], vn[:, :, :])
        for c in range(16):
            cs = slice(c * 128, (c + 1) * 128)
            p.mm([(sc_ps[:, j * 512:(j + 1) * 512], qT_bf[:, cs],
                   kT_bf[:, j * 512:(j + 1) * 512], True, True) for j in range(4)])
            p.op("vector", "reduce_max", rm[:, :], sc_ps[:, :], AX.X, negate=True)
            p.op("vector", "tensor_scalar", rmb[:, :], rm[:, :], SCALE, None, ALU.mult)
            p.op("scalar", "activation", P_sb[:, :], sc_ps[:, :], AF.Exp,
                 bias=rmb[:, :], scale=SCALE)
            p.op("vector", "reduce_sum", den[:, :], P_sb[:, :], AX.X)
            p.tr([(tp_ps[:, j, :], P_sb[:, j * 128:(j + 1) * 128], idb_sb[:, :])
                  for j in range(16)])
            p.op("vector", "tensor_copy", PT_sb[:, :, :], tp_ps[:, :, :])
            p.mm([(av_ps[:, :], PT_sb[:, j, :], v_bf[:, j, :], j == 0, j == 15)
                  for j in range(16)])
            p.op("vector", "reciprocal", dr[:, :], den[:, :])
            p.op("vector", "tensor_scalar", comb_sb[:, c, :], av_ps[:, :], dr[:, :], None, ALU.mult)
        p.dma([(comb[pr * S:(pr + 1) * S, :].rearrange("(t q) e -> q t e", q=128),
                comb_sb[:, :, :])])
    return p.finalize(st)


def build_HI():
    p = Prog()
    nc = p.nc
    st = ExitStack()
    sb = lambda n, s, d: st.enter_context(nc.sbuf_tensor(n, s, d))
    ps = lambda n, s, d: st.enter_context(nc.psum_tensor(n, s, d))

    cT = nc.dram_tensor("combT", [D, TPC], F32, kind="ExternalInput").ap()
    wo = nc.dram_tensor("wo", [D, D], F32, kind="ExternalInput").ap()
    bo = nc.dram_tensor("bo", [128, D], F32, kind="ExternalInput").ap()
    x2 = nc.dram_tensor("x2", [TPC, D], F32, kind="ExternalInput").ap()
    wt = nc.dram_tensor("wt", [D, D], F32, kind="ExternalInput").ap()
    rt = nc.dram_tensor("rt", [D, 8], F32, kind="ExternalInput").ap()
    ast = nc.dram_tensor("ast", [D, 128], F32, kind="ExternalInput").ap()
    bst = nc.dram_tensor("bst", [128, D], F32, kind="ExternalInput").ap()
    idb = nc.dram_tensor("identb", [128, 128], BF16, kind="ExternalInput").ap()
    x3d = nc.dram_tensor("x3d", [TPC, D], F32, kind="Internal").ap()
    out = nc.dram_tensor("out", [TPC, D], F32, kind="ExternalOutput").ap()

    ctmp = sb("ctmp", [128, TPC], F32)
    cbf = sb("cbf", [128, 16, TPC], BF16)
    wtmp = sb("wtmp", [128, D], F32)
    wbf = sb("wbf", [128, 16, D], BF16)
    bo_sb = sb("bo_sb", [128, D], F32)
    xt = sb("xt", [128, D], F32)
    x3_sb = sb("x3_sb", [128, D], F32)
    ln = sb("ln", [128, D], F32)
    msum = sb("msum", [128, 1], F32)
    m = sb("m", [128, 1], F32)
    vs = sb("vs", [128, 1], F32)
    sd = sb("sd", [128, 1], F32)
    rstd = sb("rstd", [128, 1], F32)
    eps_sb = sb("eps_sb", [128, 1], F32)
    ln_bf = sb("ln_bf", [128, D], BF16)
    lnT = sb("lnT", [128, 16, TPC], BF16)
    idb_sb = sb("idb_sb", [128, 128], BF16)
    rt_bf = sb("rt_bf", [128, 16, 8], BF16)
    a_bf = sb("a_bf", [128, 16, 128], BF16)
    b_bf = sb("b_bf", [128, D], BF16)
    sm_tmp = sb("sm_tmp", [128, 16, 128], F32)
    probs = sb("probs", [128, 8], F32)
    p1 = sb("p1", [128, 1], F32)
    p2 = sb("p2", [128, 1], F32)
    msk = sb("msk", [128, 8], F32)
    pm = sb("pm", [128, 8], F32)
    wr_sb = sb("wr_sb", [128, 8], F32)
    wd_sb = sb("wd_sb", [128, 8], F32)
    hw_sb = sb("hw_sb", [128, 128], BF16)
    hwT_sb = sb("hwT_sb", [128, 128], BF16)

    ff_ps = ps("ff_ps", [128, D], F32)
    tp_ps = ps("tp_ps", [128, 16, 128], BF16)
    lg_ps = ps("lg_ps", [128, 8], F32)
    h_ps = ps("h_ps", [128, 128], F32)

    p.dma([(idb_sb[:, :], idb), (bo_sb[:, :], bo)])
    p.op("vector", "memset", eps_sb[:, :], EPS)
    for i in range(16):
        p.dma([(ctmp[:, :], cT[i * 128:(i + 1) * 128, :])])
        p.op("vector", "tensor_copy", cbf[:, i, :], ctmp[:, :])
    wr = wo.rearrange("(a p) d -> p a d", p=128)
    for i in range(16):
        p.dma([(wtmp[:, :], wr[:, i, :])])
        p.op("vector", "tensor_copy", wbf[:, i, :], wtmp[:, :])

    for tt in range(8):
        ts_ = slice(tt * 128, (tt + 1) * 128)
        for ch in range(4):
            p.mm([(ff_ps[:, ch * 512:(ch + 1) * 512],
                   cbf[:, dt, ts_], wbf[:, dt, ch * 512:(ch + 1) * 512],
                   dt == 0, dt == 15) for dt in range(16)])
        p.dma([(xt[:, :], x2[ts_, :])])
        p.op("vector", "tensor_tensor", x3_sb[:, :], xt[:, :], ff_ps[:, :], ALU.add)
        p.op("vector", "tensor_tensor", x3_sb[:, :], x3_sb[:, :], bo_sb[:, :], ALU.add)
        p.dma([(x3d[ts_, :], x3_sb[:, :])])
        layernorm(p, ln[:, :], x3_sb[:, :], wtmp[:, :], msum[:, :], m[:, :],
                  vs[:, :], sd[:, :], rstd[:, :], D, eps_sb[:, :])
        p.op("vector", "tensor_copy", ln_bf[:, :], ln[:, :])
        p.tr([(tp_ps[:, j, :], ln_bf[:, j * 128:(j + 1) * 128], idb_sb[:, :])
              for j in range(16)])
        p.op("vector", "tensor_copy", lnT[:, :, ts_], tp_ps[:, :, :])

    wr2 = wt.rearrange("(a p) d -> p a d", p=128)
    for i in range(16):
        p.dma([(wtmp[:, :], wr2[:, i, :])])
        p.op("vector", "tensor_copy", wbf[:, i, :], wtmp[:, :])
    rt_r = rt.rearrange("(a p) e -> p a e", p=128)
    p.dma([(sm_tmp[:, :, 0:8], rt_r)])
    p.op("vector", "tensor_copy", rt_bf[:, :, :], sm_tmp[:, :, 0:8])
    ast_r = ast.rearrange("(a p) r -> p a r", p=128)
    p.dma([(sm_tmp[:, :, :], ast_r)])
    p.op("vector", "tensor_copy", a_bf[:, :, :], sm_tmp[:, :, :])
    p.dma([(wtmp[:, :], bst)])
    p.op("vector", "tensor_copy", b_bf[:, :], wtmp[:, :])

    for tt in range(8):
        ts_ = slice(tt * 128, (tt + 1) * 128)
        for ch in range(4):
            p.mm([(ff_ps[:, ch * 512:(ch + 1) * 512],
                   lnT[:, dt, ts_], wbf[:, dt, ch * 512:(ch + 1) * 512],
                   dt == 0, False) for dt in range(16)])
        p.mm([(lg_ps[:, :], lnT[:, dt, ts_], rt_bf[:, dt, :], dt == 0, dt == 15)
              for dt in range(16)])
        p.mm([(h_ps[:, :], lnT[:, dt, ts_], a_bf[:, dt, :], dt == 0, dt == 15)
              for dt in range(16)])
        p.op("vector", "reduce_max", p1[:, :], lg_ps[:, :], AX.X, negate=True)
        p.op("scalar", "activation", probs[:, :], lg_ps[:, :], AF.Exp, bias=p1[:, :], scale=1.0)
        p.op("vector", "reduce_sum", p2[:, :], probs[:, :], AX.X)
        p.op("vector", "reciprocal", p2[:, :], p2[:, :])
        p.op("vector", "tensor_scalar", probs[:, :], probs[:, :], p2[:, :], None, ALU.mult)
        p.op("vector", "reduce_max", p1[:, :], probs[:, :], AX.X)
        p.op("vector", "tensor_scalar", msk[:, :], probs[:, :], p1[:, :], None, ALU.is_ge)
        p.op("vector", "tensor_tensor", pm[:, :], probs[:, :], msk[:, :], ALU.mult)
        p.op("vector", "tensor_tensor", pm[:, :], probs[:, :], pm[:, :], ALU.subtract)
        p.op("vector", "reduce_max", p2[:, :], pm[:, :], AX.X)
        p.op("vector", "tensor_scalar", msk[:, :], probs[:, :], p2[:, :], None, ALU.is_ge)
        p.op("vector", "tensor_tensor", wr_sb[:, :], probs[:, :], msk[:, :], ALU.mult)
        p.op("vector", "reduce_sum", p2[:, :], wr_sb[:, :], AX.X)
        p.op("vector", "reciprocal", p2[:, :], p2[:, :])
        p.op("vector", "tensor_scalar", wd_sb[:, :], wr_sb[:, :], p2[:, :], None, ALU.mult)
        p.op("vector", "tensor_scalar", wd_sb[:, :], wd_sb[:, :], 2.0, None, ALU.mult)
        for e in range(8):
            p.op("vector", "tensor_scalar", hw_sb[:, e * 16:(e + 1) * 16],
                 h_ps[:, e * 16:(e + 1) * 16], wd_sb[:, e:e + 1], None, ALU.mult)
        p.tr([(tp_ps[:, 0, :], hw_sb[:, :], idb_sb[:, :])])
        p.op("vector", "tensor_copy", hwT_sb[:, :], tp_ps[:, 0, :])
        p.mm([(ff_ps[:, ch * 512:(ch + 1) * 512], hwT_sb[:, :],
               b_bf[:, ch * 512:(ch + 1) * 512], False, True) for ch in range(4)])
        p.dma([(xt[:, :], x3d[ts_, :])])
        p.op("vector", "tensor_tensor", ln[:, :], xt[:, :], ff_ps[:, :], ALU.add)
        p.dma([(out[ts_, :], ln[:, :])])
    return p.finalize(st)



def _catmull_grid(n_ctrl, n):
    t = np.linspace(0.0, n_ctrl - 1.0, n, dtype=np.float64)
    i = np.clip(np.floor(t).astype(np.int64), 0, n_ctrl - 2)
    f = (t - i).astype(np.float32)
    idx = np.stack([np.clip(i + k, 0, n_ctrl - 1) for k in (-1, 0, 1, 2)], axis=0)
    return idx.astype(np.int64), f


def kernel(L0_attn_hidden_states, L0_attn_Wq, L0_attn_Wk, L0_attn_Wv, L0_attn_Wo,
           L0_attn_gate, L0_ffn_gate_cp, L0_ffn_up_cp, L0_ffn_down_cp,
           L1_attn_Wqkv, L1_attn_bqkv, L1_attn_Wo, L1_attn_bo,
           L1_ffn_weight, L1_ffn_lora_A, L1_ffn_lora_B, L1_ffn_router_w):
    cid = list(range(NC))
    af32 = lambda a: np.ascontiguousarray(np.asarray(a, dtype=np.float32))

    x0 = af32(L0_attn_hidden_states).reshape(TOK, D)
    identb = np.eye(128, dtype=BF)
    identf = np.eye(128, dtype=np.float32)

    inv_freq = (1.0 / 10000.0 ** (np.arange(0, D, 2, dtype=np.float32) / D)).astype(np.float32)
    tpos = np.arange(S, dtype=np.float32)
    freqs = tpos[:, None] * inv_freq[None, :]
    emb = np.concatenate([freqs, freqs], axis=-1)
    cosf = np.cos(emb).astype(np.float32)
    sinf = np.sin(emb).astype(np.float32)

    wqt = af32(np.asarray(L0_attn_Wq).T)
    wkt = af32(np.asarray(L0_attn_Wk).T)
    wvt = af32(np.asarray(L0_attn_Wv).T)

    nc = build_A()
    maps = []
    for c in range(NC):
        srow = (c % 2) * 1024
        maps.append({
            "x": x0[c * TPC:(c + 1) * TPC],
            "cos": np.ascontiguousarray(cosf[srow:srow + 1024]),
            "sin": np.ascontiguousarray(sinf[srow:srow + 1024]),
            "wqt": wqt, "wkt": wkt, "wvt": wvt, "identb": identb,
        })
    res = run_bass_kernel_spmd(nc, maps, core_ids=cid)
    Q = np.concatenate([np.asarray(r["qo"]) for r in res.results], axis=0)
    K = np.concatenate([np.asarray(r["ko"]) for r in res.results], axis=0)
    V = np.concatenate([np.asarray(r["vo"]) for r in res.results], axis=0)

    QT = np.ascontiguousarray(Q.T)
    KT = np.ascontiguousarray(K.T)
    gv = af32(L0_attn_gate).reshape(16)
    nc = build_B()
    maps = []
    for c in range(NC):
        qs, ks, kn, vn, gb = [], [], [], [], []
        for pl in range(8):
            idx = 8 * c + pl
            b, h = idx // 16, idx % 16
            qs.append(QT[h * 128:(h + 1) * 128, b * S:(b + 1) * S])
            ks.append(KT[h * 128:(h + 1) * 128, b * S:(b + 1) * S])
            kn.append(K[b * S:(b + 1) * S, h * 128:(h + 1) * 128])
            vn.append(V[b * S:(b + 1) * S, h * 128:(h + 1) * 128])
            gb.append(np.full((128, 1), gv[h], dtype=np.float32))
        maps.append({
            "qT": np.ascontiguousarray(np.concatenate(qs, 0)),
            "kT": np.ascontiguousarray(np.concatenate(ks, 0)),
            "kN": np.ascontiguousarray(np.concatenate(kn, 0)),
            "vN": np.ascontiguousarray(np.concatenate(vn, 0)),
            "gate": np.concatenate(gb, 0),
            "identb": identb, "identf": identf,
        })
    res = run_bass_kernel_spmd(nc, maps, core_ids=cid)
    comb = np.stack([np.asarray(r["comb"]).reshape(8, S, 128) for r in res.results])
    comb = comb.reshape(4, 16, S, 128).transpose(0, 2, 1, 3).reshape(TOK, D)

    combT = np.ascontiguousarray(comb.T)
    wot = af32(np.asarray(L0_attn_Wo).T)
    nc = build_C()
    maps = [{"combT": np.ascontiguousarray(combT[:, c * TPC:(c + 1) * TPC]),
             "wot": wot, "x": x0[c * TPC:(c + 1) * TPC], "identb": identb}
            for c in range(NC)]
    res = run_bass_kernel_spmd(nc, maps, core_ids=cid)
    X1 = np.concatenate([np.asarray(r["x1"]) for r in res.results], axis=0)
    LN1T = np.concatenate([np.asarray(r["ln1t"]) for r in res.results], axis=1)
    LN1T = np.ascontiguousarray(LN1T)

    INTER = 8192
    n_ctrl = max(16, int(INTER * D / 128.9))
    idxg, fg = _catmull_grid(n_ctrl, INTER * D)
    f2 = fg * fg
    f3 = f2 * fg
    cks = [0.5 * (-fg + 2 * f2 - f3), 0.5 * (2 - 5 * f2 + 3 * f3),
           0.5 * (fg + 4 * f2 - 3 * f3), 0.5 * (-f2 + f3)]
    cks = [c.astype(np.float32) for c in cks]
    gcp = af32(L0_ffn_gate_cp)
    ucp = af32(L0_ffn_up_cp)
    dcp = af32(L0_ffn_down_cp)

    pg_c, pu_c, cg_c = [[] for _ in range(NC)], [[] for _ in range(NC)], [[] for _ in range(NC)]
    for k in range(4):
        pk_g = gcp[idxg[k]]
        pk_u = ucp[idxg[k]]
        ck = cks[k]
        for c in range(NC):
            lo, hi = c * 1024 * D, (c + 1) * 1024 * D
            pg_c[c].append(np.ascontiguousarray(pk_g[lo:hi].reshape(1024, D).T))
            pu_c[c].append(np.ascontiguousarray(pk_u[lo:hi].reshape(1024, D).T))
            cg_c[c].append(np.ascontiguousarray(ck[lo:hi].reshape(1024, D).T.astype(BF)))
    pd_c, cd_c = [[] for _ in range(NC)], [[] for _ in range(NC)]
    for k in range(4):
        pk_d = dcp[idxg[k]].reshape(D, INTER)
        ck = cks[k].reshape(D, INTER)
        for c in range(NC):
            sl = slice(c * 1024, (c + 1) * 1024)
            pd_c[c].append(np.ascontiguousarray(pk_d[:, sl].T))
            cd_c[c].append(np.ascontiguousarray(ck[:, sl].T.astype(BF)))
    nc = build_D()
    maps = [{"pg": np.concatenate(pg_c[c], 0), "cg": np.concatenate(cg_c[c], 0),
             "pu": np.concatenate(pu_c[c], 0),
             "pd": np.concatenate(pd_c[c], 0), "cd": np.concatenate(cd_c[c], 0),
             "lnt": LN1T} for c in range(NC)]
    res = run_bass_kernel_spmd(nc, maps, core_ids=cid)
    parts = [np.asarray(r["part"]) for r in res.results]

    bq = af32(L1_attn_bqkv).reshape(24, 128).T
    wqkv = af32(L1_attn_Wqkv)
    nc = build_EF()
    maps = []
    for c in range(NC):
        pstack = np.concatenate([parts[d_][c * TPC:(c + 1) * TPC] for d_ in range(NC)], 0)
        maps.append({"x1": X1[c * TPC:(c + 1) * TPC], "parts": pstack,
                     "wqkv": wqkv, "bqkv": np.ascontiguousarray(bq), "identb": identb})
    res = run_bass_kernel_spmd(nc, maps, core_ids=cid)
    X2 = np.concatenate([np.asarray(r["x2"]) for r in res.results], axis=0)
    QKVT = np.ascontiguousarray(np.concatenate([np.asarray(r["qkvt"]) for r in res.results], axis=1))

    nc = build_G()
    maps = []
    for c in range(NC):
        qs, ks, vn = [], [], []
        for pl in range(16):
            idx = 16 * c + pl
            b, qh = idx // 32, idx % 32
            kvh = qh // 4
            qs.append(QKVT[qh * 64:(qh + 1) * 64, b * S:(b + 1) * S])
            ks.append(QKVT[2048 + kvh * 64:2048 + (kvh + 1) * 64, b * S:(b + 1) * S])
            vn.append(np.ascontiguousarray(QKVT[2560 + kvh * 64:2560 + (kvh + 1) * 64, b * S:(b + 1) * S].T))
        maps.append({"qT": np.ascontiguousarray(np.concatenate(qs, 0)),
                     "kT": np.ascontiguousarray(np.concatenate(ks, 0)),
                     "vN": np.ascontiguousarray(np.concatenate(vn, 0)),
                     "identb": identb})
    res = run_bass_kernel_spmd(nc, maps, core_ids=cid)
    comb1 = np.stack([np.asarray(r["comb"]).reshape(16, S, 64) for r in res.results])
    comb1 = comb1.reshape(4, 32, S, 64).transpose(0, 2, 1, 3).reshape(TOK, D)

    combT1 = np.ascontiguousarray(comb1.T)
    wo1 = af32(L1_attn_Wo)
    bo_bc = np.ascontiguousarray(np.broadcast_to(af32(L1_attn_bo), (128, D)))
    wt_rhs = af32(np.asarray(L1_ffn_weight).T)
    rt_h = af32(np.asarray(L1_ffn_router_w).T)
    a_st = af32(np.asarray(L1_ffn_lora_A).transpose(2, 0, 1).reshape(D, 128))
    b_st = af32(np.asarray(L1_ffn_lora_B).transpose(0, 2, 1).reshape(128, D))
    nc = build_HI()
    maps = [{"combT": np.ascontiguousarray(combT1[:, c * TPC:(c + 1) * TPC]),
             "wo": wo1, "bo": bo_bc, "x2": X2[c * TPC:(c + 1) * TPC],
             "wt": wt_rhs, "rt": rt_h, "ast": a_st, "bst": b_st, "identb": identb}
            for c in range(NC)]
    res = run_bass_kernel_spmd(nc, maps, core_ids=cid)
    out = np.concatenate([np.asarray(r["out"]) for r in res.results], axis=0)
    return np.ascontiguousarray(out.reshape(4, S, D).astype(np.float32))
